# version 06
# baseline (speedup 1.0000x reference)
import numpy as np
import concourse.bass as bass
import concourse.mybir as mybir
from concourse.bass_utils import run_bass_kernel_spmd
import concourse.dve_ops as dve_ops
from concourse.dve_op import DveOp
from concourse.dve_spec import Spec, Src0, Src1, C0, C1, C2, MaxPos, maxx, lower, Bin, AluOp
from concourse.bass_isa import ReduceOp
from concourse import bass_isa

F32 = mybir.dt.float32
BF16 = mybir.dt.bfloat16
AF = mybir.ActivationFunctionType
ALU = mybir.AluOpType
AX = mybir.AxisListType

P = 128
T = 512
NT = 2048 // T
D = 16
EZ = 32
EO = 32
OD = 16
J = 44
EPS = 1e-5

MASKBITS = np.uint32(0x7F800000)
CDN = float(np.uint32(0x807FFFFF).view(np.float32))
MAGIC = 3145728.0
CLAMP = 0.015625
C448_HI = np.float32(1.0 / 448.0)
C448_LO = np.float32(np.float64(1.0) / 448.0 - np.float64(np.float32(1.0 / 448.0)))


def _np_exp_clamp(a):
    m_, e_ = np.frexp(a)
    e = (e_ - 1).astype(np.int64)
    e = np.where(a == 0, -6, e)
    return np.maximum(e, -6)


def _qp_act_ref(in0, in1, s0, s1, imm2):
    a = np.abs(np.asarray(in0))
    if a.dtype == np.float64:
        return np.ldexp(np.float64(1.0), 1 - _np_exp_clamp(a))
    b = np.ascontiguousarray(a, dtype=np.float32).view(np.uint32)
    t2 = (b & MASKBITS).view(np.float32)
    t3 = np.maximum(t2, np.float32(s1))
    return ((~np.ascontiguousarray(t3).view(np.uint32)) & MASKBITS).view(np.float32)


def _qr_act_ref(in0, in1, s0, s1, imm2):
    in0 = np.asarray(in0); in1 = np.asarray(in1)
    if in0.dtype == np.float64:
        return np.rint(in0 * in1 * 4.0) * 0.5 / in1
    m0 = (in0 * in1).astype(np.float32)
    m1 = (m0 + np.float32(s0)).astype(np.float32)
    r = (m1 - np.float32(s0)).astype(np.float32)
    p3 = ((~np.ascontiguousarray(in1, dtype=np.float32).view(np.uint32)) & MASKBITS).view(np.float32)
    return (r * p3).astype(np.float32)


def _qp_w_ref(in0, in1, s0, s1, imm2):
    in0 = np.asarray(in0)
    if in0.dtype == np.float64:
        u = np.abs(in0 * np.asarray(s0))
        return np.ldexp(np.float64(1.0), 1 - _np_exp_clamp(u))
    a = np.abs((in0 * np.asarray(s0, dtype=np.float32)).astype(np.float32))
    b = np.ascontiguousarray(a).view(np.uint32)
    t2 = (b & MASKBITS).view(np.float32)
    t3 = np.maximum(t2, np.float32(imm2))
    return ((~np.ascontiguousarray(t3).view(np.uint32)) & MASKBITS).view(np.float32)


def _qr_w_ref(in0, in1, s0, s1, imm2):
    in0 = np.asarray(in0); in1 = np.asarray(in1)
    if in0.dtype == np.float64:
        ws = np.asarray(s1) * (np.float64(1.0) / 448.0)
        u = in0 / ws
        return np.rint(u * in1 * 4.0) * 0.5 / in1
    u = (in0 * np.asarray(s0, dtype=np.float32)).astype(np.float32)
    m0 = (u * in1).astype(np.float32)
    m1 = (m0 + np.float32(imm2)).astype(np.float32)
    r = (m1 - np.float32(imm2)).astype(np.float32)
    p3 = ((~np.ascontiguousarray(in1, dtype=np.float32).view(np.uint32)) & MASKBITS).view(np.float32)
    return (r * p3).astype(np.float32)


_REPLACE_SLOTS = ["SHIFT_RIGHT_TT_BV", "DVEHASH32L_SEED", "DVEHASH32L_BODY",
                  "DVEHASH32L_DRAIN"]


def _mk_op(name, body, ref):
    for op in dve_ops.OPS:
        if op.name == name:
            return op
    spec = Spec(body=body)
    shas = {}
    for ver in ("v3", "v4"):
        try:
            shas[ver] = lower(spec, ver=ver, perf_en=False, name=name).sha(ver)
        except Exception:
            pass
    op = DveOp(name, spec, subdim=False, uops_sha=shas, reference=ref)
    victim = _REPLACE_SLOTS.pop(0)
    idx = next(i for i, o in enumerate(dve_ops.OPS) if o.name == victim)
    dve_ops.OPS[idx] = op
    dve_ops._SUB_OPCODE_FOR_NAME[name] = dve_ops._CUSTOM_DVE_ROW_BASE + idx
    dve_ops.CUSTOM_DVE_REFS[name] = op.ref
    dve_ops.CUSTOM_DVE_SPECS[name] = spec
    return op


def _register_ops():
    mask = Bin(AluOp.BITWISE_NOT, C0, C0)
    t2 = Bin(AluOp.BITWISE_AND, Src0, mask)
    t3 = maxx(t2, C1)
    n1 = Bin(AluOp.BITWISE_NOT, t3, t3)
    qp_act = _mk_op("ANT_QPA", Bin(AluOp.BITWISE_AND, n1, mask), _qp_act_ref)

    mask2 = Bin(AluOp.BITWISE_NOT, C1, C1)
    m0 = Src0 * Src1
    m1 = m0 + C0
    r = m1 - C0
    pn = Bin(AluOp.BITWISE_NOT, Src1, Src1)
    p3 = Bin(AluOp.BITWISE_AND, pn, mask2)
    qr_act = _mk_op("ANT_QRA", r * p3, _qr_act_ref)

    mask3 = Bin(AluOp.BITWISE_NOT, C1, C1)
    tw = Src0 * C0
    t2w = Bin(AluOp.BITWISE_AND, tw, mask3)
    t3w = maxx(t2w, C2)
    n1w = Bin(AluOp.BITWISE_NOT, t3w, t3w)
    qp_w = _mk_op("ANT_QPW", Bin(AluOp.BITWISE_AND, n1w, mask3), _qp_w_ref)

    maskw = Bin(AluOp.ADD, MaxPos, MaxPos)
    tw2 = Src0 * C0
    m0w = tw2 * Src1
    m1w = m0w + C2
    rw = m1w - C2
    pnw = Bin(AluOp.BITWISE_NOT, Src1, Src1)
    p3w = Bin(AluOp.BITWISE_AND, pnw, maskw)
    qr_w = _mk_op("ANT_QRW", rw * p3w, _qr_w_ref)
    return qp_act, qr_act, qp_w, qr_w


QP_ACT, QR_ACT, QP_W, QR_W = _register_ops()


class Sched:

    def __init__(self):
        self.steps = []
        self.counts = {}

    def add(self, engine, emit, waits=(), inc=None):
        w = [(s, v) for (s, v) in waits if s is not None and v > 0]
        tok = None
        if inc is not None:
            sem, amt = inc
            key = id(sem)
            self.counts[key] = self.counts.get(key, 0) + amt
            tok = (sem, self.counts[key])
        self.steps.append((engine, tuple(w), emit, inc))
        return tok

    def now(self, sem):
        return (sem, self.counts.get(id(sem), 0))

    def replay(self, engine, eng):
        last_wait = {}
        for (e, waits, emit, inc) in self.steps:
            if e != engine:
                continue
            for (sem, val) in waits:
                if last_wait.get(id(sem), -1) < val:
                    eng.wait_ge(sem, val)
                    last_wait[id(sem)] = val
            inst = emit(eng)
            if inc is not None:
                sem, amt = inc
                inst.then_inc(sem, amt)


_LAST_S = None


def build_nc():
    global _LAST_S
    nc = bass.Bass("TRN2", target_bir_lowering=False)

    xT = nc.dram_tensor("xT", [2048, 2048], F32, kind="ExternalInput").ap()
    WinT = nc.dram_tensor("WinT", [2048, 8192], F32, kind="ExternalInput").ap()
    WoutT = nc.dram_tensor("WoutT", [4096, 2048], F32, kind="ExternalInput").ap()
    wg2 = nc.dram_tensor("wg2", [5632, 2048], F32, kind="ExternalInput").ap()
    wu2 = nc.dram_tensor("wu2", [5632, 2048], F32, kind="ExternalInput").ap()
    wdn = nc.dram_tensor("wdn", [2048, 5632], F32, kind="ExternalInput").ap()
    gamT = nc.dram_tensor("gamT", [128, 16], F32, kind="ExternalInput").ap()
    betT = nc.dram_tensor("betT", [128, 16], F32, kind="ExternalInput").ap()
    dtT = nc.dram_tensor("dtT", [128, 32], F32, kind="ExternalInput").ap()
    identD = nc.dram_tensor("identD", [128, 128], BF16, kind="ExternalInput").ap()
    outT = nc.dram_tensor("outT", [2048, 2048], F32, kind="ExternalOutput").ap()

    cWin = nc.dram_tensor("cWin", [64, 128, D, 128], BF16, kind="Internal").ap()
    cWout = nc.dram_tensor("cWout", [OD, 128, EO, 128], BF16, kind="Internal").ap()
    cg = nc.dram_tensor("cg", [J, 128, D, 128], BF16, kind="Internal").ap()
    cu = nc.dram_tensor("cu", [J, 128, D, 128], BF16, kind="Internal").ap()
    cd = nc.dram_tensor("cd", [OD, 128, J, 128], BF16, kind="Internal").ap()
    tbuf = nc.dram_tensor("tbuf", [J, 128, T], F32, kind="Internal").ap()

    S = Sched()
    _LAST_S = S
    ctx = []

    def sb(name, shape, dtype):
        h = nc.sbuf_tensor(name, shape, dtype)
        t = h.__enter__()
        ctx.append(h)
        return t

    def ps(name, shape, dtype):
        h = nc.psum_tensor(name, shape, dtype)
        t = h.__enter__()
        ctx.append(h)
        return t

    ident = sb("ident", [P, 128], BF16)
    gam = sb("gam", [P, D], F32)
    ngam = sb("ngam", [P, D], F32)
    bet = sb("bet", [P, D], F32)
    dtt = sb("dtt", [P, 32], F32)
    dtw = sb("dtw", [P, 32], F32)
    wsg = sb("wsg", [P, J], F32)
    wsu = sb("wsu", [P, J], F32)
    wsd = sb("wsd", [P, OD], F32)

    vmax = sb("vmax", [P, 1], F32)
    vtmp = sb("vtmp", [P, 1], F32)
    vws = sb("vws", [P, 1], F32)
    vinv = sb("vinv", [P, 1], F32)

    pctx = []

    def sbp(name, shape, dtype):
        h = nc.sbuf_tensor(name, shape, dtype)
        t = h.__enter__()
        pctx.append(h)
        return t

    wband = [sbp(f"wband{i}", [P, 4096], F32) for i in range(2)]
    wband2 = [sbp(f"wband2{i}", [P, 5632], F32) for i in range(2)]
    bfband = sbp("bfband", [P, 8192], BF16)
    qband = sbp("qband", [P, 5632], BF16)
    pband = sbp("pband", [P, 5632], F32)
    trev = [sbp(f"trev{i}", [P, 128], BF16) for i in range(2)]

    pz = ps("pz", [P, T], F32)
    pg = ps("pg", [P, T], F32)
    py = [ps(f"py{i}", [P, T], F32) for i in range(2)]
    pgu = [ps(f"pgu{i}", [P, T], F32) for i in range(2)]
    ptr = [ps(f"ptr{i}", [P, 128], BF16) for i in range(2)]

    sems = {}
    for name in ["sv", "ss", "tt", "sg"]:
        h = nc.semaphore(name)
        sems[name] = h.__enter__()
        ctx.append(h)
    sv, ss, tt, sg = (sems[k] for k in ["sv", "ss", "tt", "sg"])

    dma_sems = {}

    def dma_sem(key):
        if key not in dma_sems:
            h = nc.semaphore(f"q_{key}")
            dma_sems[key] = h.__enter__()
            ctx.append(h)
        return dma_sems[key]

    def load(out, in_, key, waits=()):
        return S.add("sync", (lambda o, i: (lambda e: e.dma_start(out=o, in_=i)))(out, in_),
                     waits=waits, inc=(dma_sem(key), 16))

    def store(out, in_, key, waits=()):
        return S.add("scalar", (lambda o, i: (lambda e: e.dma_start(out=o, in_=i)))(out, in_),
                     waits=waits, inc=(dma_sem(key), 16))

    def _chain(eng_sem, waits):
        w = list(waits)
        if S.counts.get(id(eng_sem), 0) > 0:
            w.append(S.now(eng_sem))
        return w

    def vop(emit, waits=()):
        return S.add("vector", emit, waits=_chain(sv, waits), inc=(sv, 1))

    def sop(emit, waits=()):
        return S.add("scalar", emit, waits=_chain(ss, waits), inc=(ss, 1))

    def top(emit, waits=()):
        return S.add("tensor", emit, waits=waits, inc=(tt, 1))

    def gop(emit, waits=()):
        return S.add("gpsimd", emit, waits=_chain(sg, waits), inc=(sg, 1))

    tk_ident = load(ident[:], identD, "ident")
    tk_gam = load(gam[:], gamT, "gam")
    tk_bet = load(bet[:], betT, "bet")
    tk_dt = load(dtt[:], dtT, "dt")
    tk_ngam = vop(lambda e: e.tensor_scalar(out=ngam[:], in0=gam[:], scalar1=-1.0, scalar2=None, op0=ALU.mult), waits=[tk_gam])

    t1 = vop(lambda e: e.tensor_scalar(out=dtw[:], in0=dtt[:], scalar1=-1.0, scalar2=None, op0=ALU.mult), waits=[tk_dt])
    t2 = vop(lambda e: e.tensor_reduce(out=vmax[:], in_=dtw[:], axis=AX.X, op=ALU.max), waits=[t1])
    t3 = gop(lambda e: e.partition_all_reduce(vtmp[:], vmax[:], channels=128, reduce_op=ReduceOp.max), waits=[t2])
    t4 = vop(lambda e: e.tensor_scalar(out=vtmp[:], in0=vtmp[:], scalar1=-1.0, scalar2=None, op0=ALU.mult), waits=[t3])
    t5 = sop(lambda e: e.activation(out=dtw[:], in_=dtw[:], func=AF.Exp, bias=vtmp[:, 0:1]), waits=[t4])
    t6 = vop(lambda e: e.tensor_reduce(out=vmax[:], in_=dtw[:], axis=AX.X, op=ALU.add), waits=[t5])
    t7 = gop(lambda e: e.partition_all_reduce(vtmp[:], vmax[:], channels=128, reduce_op=ReduceOp.add), waits=[t6])
    t8 = vop(lambda e: e.reciprocal(out=vtmp[:], in_=vtmp[:]), waits=[t7])
    tk_dtw = vop(lambda e: e.tensor_scalar(out=dtw[:], in0=dtw[:], scalar1=vtmp[:, 0:1], scalar2=None, op0=ALU.mult), waits=[t8])

    prev_cast = {}
    prev_store = None
    for c in range(D):
        for h in range(2):
            k = (c * 2 + h) % 2
            buf = wband[k]
            w = [prev_cast.get(k)] if prev_cast.get(k) else []
            tk_l = load(buf[:], WinT[c * 128:(c + 1) * 128, h * 4096:(h + 1) * 4096], f"wband{k}", waits=w)
            wcast = [tk_l] + ([prev_store] if h == 0 and prev_store else [])
            tk_c = vop(lambda e, b=buf, hh=h: e.tensor_copy(out=bfband[:, hh * 4096:(hh + 1) * 4096], in_=b[:]), waits=wcast)
            prev_cast[k] = tk_c
        tk_c2 = prev_cast[(c * 2 + 1) % 2]
        for E in range(64):
            prev_store = store(cWin[E, :, c, :], bfband[:, E * 128:(E + 1) * 128], "bfband", waits=[tk_c2])

    for c in range(EO):
        k = c % 2
        buf = wband[k]
        w = [prev_cast.get(k)] if prev_cast.get(k) else []
        tk_l = load(buf[:, 0:2048], WoutT[c * 128:(c + 1) * 128, :], f"wband{k}", waits=w)
        tk_c = vop(lambda e, b=buf: e.tensor_copy(out=bfband[:, 0:2048], in_=b[:, 0:2048]), waits=[tk_l, prev_store] if prev_store else [tk_l])
        prev_cast[k] = tk_c
        for O in range(OD):
            prev_store = store(cWout[O, :, c, :], bfband[:, O * 128:(O + 1) * 128], "bfband", waits=[tk_c])

    last_tr = {0: None, 1: None}
    last_ev = {0: None, 1: None}
    last_evst = {0: None, 1: None}
    prev_q = None
    prev_ld2 = {0: None, 1: None}

    def quant_weight_matrix(src, nbands, width, ws_cols, cache):
        nonlocal prev_q
        nin = width // 128
        for b in range(nbands):
            k = b % 2
            buf = wband2[k]
            w = [prev_ld2[k]] if prev_ld2[k] else []
            tk_l = load(buf[:, 0:width], src[b * 128:(b + 1) * 128, :], f"wband2{k}", waits=w)
            tk_m = vop(lambda e, bb=buf, wd=width: e.tensor_reduce(out=vmax[:], in_=bb[:, 0:wd], axis=AX.X, op=ALU.max, apply_absolute_value=True), waits=[tk_l])
            tk_a = vop(lambda e: e.tensor_scalar(out=vws[:], in0=vmax[:], scalar1=float(C448_HI), scalar2=None, op0=ALU.mult), waits=[tk_m])
            tk_b = vop(lambda e: e.tensor_scalar(out=vtmp[:], in0=vmax[:], scalar1=float(C448_LO), scalar2=None, op0=ALU.mult), waits=[tk_m])
            tk_w = vop(lambda e: e.tensor_tensor(out=vws[:], in0=vws[:], in1=vtmp[:], op=ALU.add), waits=[tk_a, tk_b])
            vop(lambda e, col=ws_cols[b]: e.tensor_copy(out=col, in_=vws[:]), waits=[tk_w])
            tk_i = vop(lambda e: e.reciprocal(out=vinv[:], in_=vws[:]), waits=[tk_w])
            wq = [tk_i] + [x for x in (last_tr[0], last_tr[1]) if x]
            tk_p = vop(lambda e, bb=buf, wd=width: e._custom_dve(QP_W, out=pband[:, 0:wd], in0=bb[:, 0:wd], s0=vinv[:, 0:1], s1=CDN, imm2=CLAMP), waits=wq)
            tk_q = vop(lambda e, bb=buf, wd=width: e._custom_dve(QR_W, out=qband[:, 0:wd], in0=bb[:, 0:wd], in1=pband[:, 0:wd], s0=vinv[:, 0:1], s1=vmax[:, 0:1], imm2=MAGIC), waits=[tk_p])
            prev_q = tk_q
            prev_ld2[k] = tk_q
            for c in range(nin):
                bk = c % 2
                wtr = [tk_q, tk_ident] + ([last_ev[bk]] if last_ev[bk] else [])
                tk_t = top(lambda e, cc=c, bank=ptr[bk]: e.transpose(out=bank[:], in_=qband[:, cc * 128:(cc + 1) * 128], identity=ident[:]), waits=wtr)
                last_tr[bk] = tk_t
                wev = [tk_t] + ([last_evst[bk]] if last_evst[bk] else [])
                tk_e = vop(lambda e, bank=ptr[bk], evv=trev[bk]: e.tensor_copy(out=evv[:], in_=bank[:]), waits=wev)
                last_ev[bk] = tk_e
                last_evst[bk] = store(cache[b, :, c, :], trev[bk][:], f"trev{bk}", waits=[tk_e])

    wsg_cols = [wsg[:, j:j + 1] for j in range(J)]
    wsu_cols = [wsu[:, j:j + 1] for j in range(J)]
    wsd_cols = [wsd[:, o:o + 1] for o in range(OD)]
    quant_weight_matrix(wg2, J, 2048, wsg_cols, cg)
    quant_weight_matrix(wu2, J, 2048, wsu_cols, cu)
    quant_weight_matrix(wdn, OD, 5632, wsd_cols, cd)

    prolog_done = [S.now(sv), S.now(tt), S.now(ss), S.now(sg),
                   S.now(dma_sem("bfband")), S.now(dma_sem("trev0")),
                   S.now(dma_sem("trev1"))]

    for h in reversed(pctx):
        h.__exit__(None, None, None)

    slabz = [sb(f"slabz{i}", [P, D * 128], BF16) for i in range(2)]
    slabg = [sb(f"slabg{i}", [P, D * 128], BF16) for i in range(2)]
    slabo = [sb(f"slabo{i}", [P, 2048], BF16) for i in range(2)]
    slabgu = [sb(f"slabgu{i}", [P, D * 128], BF16) for i in range(4)]
    slabd = [sb(f"slabd{i}", [P, 22 * 128], BF16) for i in range(2)]

    xa = [sb(f"xa{c}", [P, T], F32) for c in range(D)]
    xn0 = [sb(f"xn0{c}", [P, T], F32) for c in range(D)]
    xnb = [sb(f"xnb{c}", [P, T], BF16) for c in range(D)]
    qt = [sb(f"qt{j}", [P, T], BF16) for j in range(J)]
    outp = qt[:EO]
    x1 = xn0
    xn2 = xa
    qx = xnb
    tch = [sb(f"tch{i}", [P, T], F32) for i in range(2)]
    uch = sb("uch", [P, T], F32)
    pch = sb("pch", [P, T], F32)

    bsum = sb("bsum", [P, T], F32)
    bsq = sb("bsq", [P, T], F32)
    sqt = sb("sqt", [P, T], F32)
    btmp2 = sqt
    bmu = sb("bmu", [P, T], F32)
    brstd = sb("brstd", [P, T], F32)
    bmurstd = sb("bmurstd", [P, T], F32)
    bmu2, brstd2, bm2r2 = bmu, brstd, bmurstd
    brm = sb("brm", [P, T], F32)
    binv = sb("binv", [P, T], F32)
    bscg = sb("bscg", [P, T], F32)
    btmp = sb("btmp", [P, T], F32)
    zcb = bsum
    sgb = bsq
    ztb = brm
    gcb = binv
    gsb = gcb
    gscb = bmu
    uscb = bmurstd

    def ln_stats(src_chunks, mu_t, rstd_t, murstd_t):
        toks = []
        tk_lastq = None
        for c in range(D):
            tk_sq = sop(lambda e, cc=c: e.activation(out=sqt[:], in_=src_chunks[cc][0][:], func=AF.Square),
                        waits=[src_chunks[c][1]] + ([tk_lastq] if tk_lastq else []))
            if c == 0:
                tk_s = vop(lambda e, cc=c: e.tensor_copy(out=bsum[:], in_=src_chunks[cc][0][:]), waits=[src_chunks[c][1]])
                tk_qq = vop(lambda e: e.tensor_copy(out=bsq[:], in_=sqt[:]), waits=[tk_sq])
            else:
                tk_s = vop(lambda e, cc=c: e.tensor_tensor(out=bsum[:], in0=bsum[:], in1=src_chunks[cc][0][:], op=ALU.add), waits=[src_chunks[c][1]])
                tk_qq = vop(lambda e: e.tensor_tensor(out=bsq[:], in0=bsq[:], in1=sqt[:], op=ALU.add), waits=[tk_sq])
            tk_lastq = tk_qq
            toks.append((tk_s, tk_qq))
        tk_ps = gop(lambda e: e.partition_all_reduce(bsum[:], bsum[:], channels=128, reduce_op=ReduceOp.add), waits=[toks[-1][0]])
        tk_pq = gop(lambda e: e.partition_all_reduce(bsq[:], bsq[:], channels=128, reduce_op=ReduceOp.add), waits=[toks[-1][1]])
        tk_mu = vop(lambda e: e.tensor_scalar(out=mu_t[:], in0=bsum[:], scalar1=float(np.float32(1.0 / 2048.0)), scalar2=None, op0=ALU.mult), waits=[tk_ps])
        tk_ms = vop(lambda e: e.tensor_scalar(out=bsq[:], in0=bsq[:], scalar1=float(np.float32(1.0 / 2048.0)), scalar2=None, op0=ALU.mult), waits=[tk_pq])
        tk_m2 = vop(lambda e: e.tensor_tensor(out=btmp2[:], in0=mu_t[:], in1=mu_t[:], op=ALU.mult), waits=[tk_mu])
        tk_var = vop(lambda e: e.tensor_tensor(out=btmp2[:], in0=bsq[:], in1=btmp2[:], op=ALU.subtract), waits=[tk_ms, tk_m2])
        tk_ln = sop(lambda e: e.activation(out=btmp2[:], in_=btmp2[:], func=AF.Ln, bias=np.float32(EPS)), waits=[tk_var])
        tk_rs = sop(lambda e: e.activation(out=rstd_t[:], in_=btmp2[:], func=AF.Exp, scale=np.float32(-0.5)), waits=[tk_ln])
        tk_mr = vop(lambda e: e.tensor_tensor(out=murstd_t[:], in0=mu_t[:], in1=rstd_t[:], op=ALU.mult), waits=[tk_rs])
        return tk_rs, tk_mr

    carry_T_last = None
    carry_out_st = [None] * OD
    carry_x1_ev = [None] * D
    prev_z_evac = None
    prev_g_evac = None
    prev_gu_evac = None
    py_free = [None, None]
    mm2_half_tok = [None, None]
    dn_half_tok = [None, None]

    for tile in range(NT):
        t0 = tile * T
        xld = None
        for c in range(D):
            w = ([carry_T_last] if carry_T_last else []) +                 ([carry_x1_ev[c]] if carry_x1_ev[c] else []) +                 ([carry_out_st[c]] if carry_out_st[c] else []) +                 (prolog_done if tile == 0 else [])
            xld = load(xa[c][:], xT[c * 128:(c + 1) * 128, t0:t0 + T], "xa", waits=w)
        xtok = [(xa[c], xld) for c in range(D)]

        w0 = ([carry_T_last] if carry_T_last else []) + [x for x in carry_out_st if x] +              (prolog_done if tile == 0 else [])
        if w0:
            vop(lambda e: e.nop(), waits=w0)

        tk_rs1, tk_mr1 = ln_stats(xtok, bmu, brstd, bmurstd)
        xn0tok = []
        for c in range(D):
            tk1 = vop(lambda e, cc=c: e.tensor_tensor(out=xn0[cc][:], in0=xa[cc][:], in1=brstd[:], op=ALU.mult), waits=[xtok[c][1], tk_rs1])
            tk2 = vop(lambda e, cc=c: e.tensor_tensor(out=xn0[cc][:], in0=xn0[cc][:], in1=bmurstd[:], op=ALU.subtract), waits=[tk_mr1])
            xn0tok.append((xn0[c], tk2))
        tk_rs2, tk_mr2 = ln_stats(xn0tok, bmu2, brstd2, bm2r2)
        xnbtok = []
        for c in range(D):
            tka = vop(lambda e, cc=c: e.scalar_tensor_tensor(out=btmp[:], in0=xn0[cc][:], scalar=gam[:, cc:cc + 1], in1=brstd2[:], op0=ALU.mult, op1=ALU.mult), waits=[tk_rs2, tk_gam])
            tkb = vop(lambda e, cc=c: e.affine_then_add(out=xnb[cc][:], in0=bm2r2[:], in1=btmp[:], scale=ngam[:, cc:cc + 1], bias=bet[:, cc:cc + 1]), waits=[tka, tk_mr2, tk_bet, tk_ngam])
            xnbtok.append((xnb[c], tkb))
        xstok = []
        for c in range(D):
            tk = vop(lambda e, cc=c: e.tensor_tensor(out=xa[cc][:], in0=xa[cc][:], in1=xn0[cc][:], op=ALU.add), waits=[xnbtok[c][1]])
            xstok.append((xa[c], tk))

        outptok = []
        for ep in range(EZ):
            bz = slabz[ep % 2]
            bg = slabg[ep % 2]
            wz = [prev_z_evac] if prev_z_evac else []
            wg_ = [prev_g_evac] if prev_g_evac else []
            tk_lz = load(bz[:].rearrange("p (c e) -> p c e", e=128), cWin[ep], f"slabz{ep % 2}", waits=wz)
            tk_lg = load(bg[:].rearrange("p (c e) -> p c e", e=128), cWin[ep + 32], f"slabg{ep % 2}", waits=wg_)
            tk_mz = None
            for c in range(D):
                tk_mz = top(lambda e, cc=c, bb=bz: e.matmul(pz[:], bb[:, cc * 128:(cc + 1) * 128], xnb[cc][:], start=(cc == 0), stop=(cc == D - 1)),
                            waits=([tk_lz, xnbtok[-1][1]] + wz if c == 0 else []))
            tk_mg = None
            for c in range(D):
                tk_mg = top(lambda e, cc=c, bb=bg: e.matmul(pg[:], bb[:, cc * 128:(cc + 1) * 128], xnb[cc][:], start=(cc == 0), stop=(cc == D - 1)),
                            waits=([tk_lg] + wg_ if c == 0 else []))
            tk_zc = vop(lambda e: e.tensor_scalar(out=zcb[:], in0=pz[:], scalar1=15.0, scalar2=-15.0, op0=ALU.min, op1=ALU.max), waits=[tk_mz])
            tk_sg = sop(lambda e: e.activation(out=sgb[:], in_=zcb[:], func=AF.Sigmoid), waits=[tk_zc])
            tk_zt = vop(lambda e: e.tensor_tensor(out=ztb[:], in0=pz[:], in1=sgb[:], op=ALU.mult), waits=[tk_sg])
            prev_z_evac = tk_zt
            tk_gc = vop(lambda e: e.tensor_scalar(out=gcb[:], in0=pg[:], scalar1=15.0, scalar2=-15.0, op0=ALU.min, op1=ALU.max), waits=[tk_mg])
            tk_gs = sop(lambda e: e.activation(out=gsb[:], in_=gcb[:], func=AF.Silu), waits=[tk_gc])
            tk_o = vop(lambda e, eep=ep: e.scalar_tensor_tensor(out=outp[eep][:], in0=ztb[:], scalar=dtw[:, eep:eep + 1], in1=gsb[:], op0=ALU.mult, op1=ALU.mult), waits=[tk_gs, tk_dtw])
            prev_g_evac = tk_o
            outptok.append((outp[ep], tk_o))

        x1tok = []
        for o in range(OD):
            bank = py[o % 2]
            wts = [py_free[o % 2]] if py_free[o % 2] else []
            wlo0 = [mm2_half_tok[0]] if mm2_half_tok[0] else []
            wlo1 = [mm2_half_tok[1]] if mm2_half_tok[1] else []
            tk_lo0 = load(slabo[0][:].rearrange("p (c e) -> p c e", e=128), cWout[o, :, 0:16, :], "slabo0", waits=wlo0)
            tk_lo1 = load(slabo[1][:].rearrange("p (c e) -> p c e", e=128), cWout[o, :, 16:32, :], "slabo1", waits=wlo1)
            tk_m = None
            for c in range(EO):
                w_mm = []
                if c == 0:
                    w_mm = [tk_lo0, outptok[-1][1]] + wts
                elif c == 16:
                    w_mm = [tk_lo1]
                tk_m = top(lambda e, cc=c, bk=bank: e.matmul(bk[:], slabo[cc // 16][:, (cc % 16) * 128:((cc % 16) + 1) * 128], outp[cc][:], start=(cc == 0), stop=(cc == EO - 1)),
                           waits=w_mm)
                if c == 15:
                    mm2_half_tok[0] = tk_m
            mm2_half_tok[1] = tk_m
            tk_x1 = vop(lambda e, oo=o, bk=bank: e.tensor_tensor(out=x1[oo][:], in0=bk[:], in1=xa[oo][:], op=ALU.add), waits=[tk_m, xstok[o][1]])
            py_free[o % 2] = tk_x1
            x1tok.append((x1[o], tk_x1))
            carry_x1_ev[o] = tk_x1

        tk_rs3, tk_mr3 = ln_stats(x1tok, bmu, brstd, bmurstd)
        xn2tok = []
        for c in range(D):
            w = [x1tok[c][1], tk_rs3] + ([carry_out_st[c]] if carry_out_st[c] else [])
            tk1 = vop(lambda e, cc=c: e.tensor_tensor(out=xn2[cc][:], in0=x1[cc][:], in1=brstd[:], op=ALU.mult), waits=w)
            tk2 = vop(lambda e, cc=c: e.tensor_tensor(out=xn2[cc][:], in0=xn2[cc][:], in1=bmurstd[:], op=ALU.subtract), waits=[tk_mr3])
            xn2tok.append((xn2[c], tk2))
        tk_cm = None
        tk_cn = None
        for c in range(D):
            if c == 0:
                tk_cm = vop(lambda e: e.tensor_copy(out=brm[:], in_=xn2[0][:]), waits=[xn2tok[0][1]])
                tk_cn = vop(lambda e: e.tensor_copy(out=bsum[:], in_=xn2[0][:]))
            else:
                tk_cm = vop(lambda e, cc=c: e.tensor_tensor(out=brm[:], in0=brm[:], in1=xn2[cc][:], op=ALU.max), waits=[xn2tok[c][1]])
                tk_cn = vop(lambda e, cc=c: e.tensor_tensor(out=bsum[:], in0=bsum[:], in1=xn2[cc][:], op=ALU.min))
        tk_cg = vop(lambda e: e.tensor_scalar(out=bsum[:], in0=bsum[:], scalar1=-1.0, scalar2=None, op0=ALU.mult), waits=[tk_cn])
        tk_cm = vop(lambda e: e.tensor_tensor(out=brm[:], in0=brm[:], in1=bsum[:], op=ALU.max), waits=[tk_cm, tk_cg])
        tk_pm = gop(lambda e: e.partition_all_reduce(brm[:], brm[:], channels=128, reduce_op=ReduceOp.max), waits=[tk_cm])
        tk_iv = vop(lambda e: e.reciprocal(out=binv[:], in_=brm[:]), waits=[tk_pm])
        tk_iv2 = vop(lambda e: e.tensor_scalar(out=binv[:], in0=binv[:], scalar1=448.0, scalar2=None, op0=ALU.mult), waits=[tk_iv])
        tk_sa = vop(lambda e: e.tensor_scalar(out=bscg[:], in0=brm[:], scalar1=float(np.float32(C448_HI) * np.float32(0.25)), scalar2=None, op0=ALU.mult), waits=[tk_pm])
        tk_sb = vop(lambda e: e.tensor_scalar(out=btmp[:], in0=brm[:], scalar1=float(np.float32(C448_LO) * np.float32(0.25)), scalar2=None, op0=ALU.mult), waits=[tk_pm])
        tk_sc = vop(lambda e: e.tensor_tensor(out=bscg[:], in0=bscg[:], in1=btmp[:], op=ALU.add), waits=[tk_sa, tk_sb])
        qxtok = []
        for c in range(D):
            tku = vop(lambda e, cc=c: e.tensor_tensor(out=uch[:], in0=xn2[cc][:], in1=binv[:], op=ALU.mult), waits=[tk_iv2])
            tkp = vop(lambda e: e._custom_dve(QP_ACT, out=pch[:], in0=uch[:], s0=CDN, s1=CLAMP), waits=[tku])
            tkq = vop(lambda e, cc=c: e._custom_dve(QR_ACT, out=qx[cc][:], in0=uch[:], in1=pch[:], s0=MAGIC, s1=CDN), waits=[tkp])
            qxtok.append((qx[c], tkq))

        tk_tm = None
        tstore = []
        for j in range(J):
            bgu = slabgu[(j % 2) * 2]
            buu = slabgu[(j % 2) * 2 + 1]
            wts = [prev_gu_evac] if prev_gu_evac else []
            tk_lg2 = load(bgu[:].rearrange("p (c e) -> p c e", e=128), cg[j], f"slabgu{(j % 2) * 2}", waits=wts)
            tk_lu2 = load(buu[:].rearrange("p (c e) -> p c e", e=128), cu[j], f"slabgu{(j % 2) * 2 + 1}", waits=wts)
            tk_mg2 = None
            for c in range(D):
                tk_mg2 = top(lambda e, cc=c, bb=bgu: e.matmul(pgu[0][:], bb[:, cc * 128:(cc + 1) * 128], qx[cc][:], start=(cc == 0), stop=(cc == D - 1)),
                             waits=([tk_lg2, qxtok[-1][1]] + wts if c == 0 else []))
            tk_mu2 = None
            for c in range(D):
                tk_mu2 = top(lambda e, cc=c, bb=buu: e.matmul(pgu[1][:], bb[:, cc * 128:(cc + 1) * 128], qx[cc][:], start=(cc == 0), stop=(cc == D - 1)),
                             waits=([tk_lu2] + wts if c == 0 else []))
            tk_gsc = vop(lambda e, jj=j: e.scalar_tensor_tensor(out=gscb[:], in0=pgu[0][:], scalar=wsg[:, jj:jj + 1], in1=bscg[:], op0=ALU.mult, op1=ALU.mult), waits=[tk_mg2, tk_sc])
            tk_usc = vop(lambda e, jj=j: e.scalar_tensor_tensor(out=uscb[:], in0=pgu[1][:], scalar=wsu[:, jj:jj + 1], in1=bscg[:], op0=ALU.mult, op1=ALU.mult), waits=[tk_mu2])
            prev_gu_evac = tk_usc
            tk_gsl = sop(lambda e: e.activation(out=gscb[:], in_=gscb[:], func=AF.Silu), waits=[tk_gsc])
            tc = tch[j % 2]
            tk_t = vop(lambda e, tcc=tc: e.tensor_tensor(out=tcc[:], in0=gscb[:], in1=uscb[:], op=ALU.mult),
                       waits=[tk_gsl] + ([tstore[j - 2]] if j >= 2 else []))
            if j == 0:
                tk_tm = vop(lambda e, tcc=tc: e.tensor_copy(out=brm[:], in_=tcc[:]), waits=[tk_t])
                tk_tn = vop(lambda e, tcc=tc: e.tensor_copy(out=bsum[:], in_=tcc[:]))
            else:
                tk_tm = vop(lambda e, tcc=tc: e.tensor_tensor(out=brm[:], in0=brm[:], in1=tcc[:], op=ALU.max), waits=[tk_t])
                tk_tn = vop(lambda e, tcc=tc: e.tensor_tensor(out=bsum[:], in0=bsum[:], in1=tcc[:], op=ALU.min))
            tstore.append(store(tbuf[j], tc[:], f"tchst{j % 2}", waits=[tk_tn]))

        tk_tg = vop(lambda e: e.tensor_scalar(out=bsum[:], in0=bsum[:], scalar1=-1.0, scalar2=None, op0=ALU.mult), waits=[tk_tn])
        tk_tmx = vop(lambda e: e.tensor_tensor(out=brm[:], in0=brm[:], in1=bsum[:], op=ALU.max), waits=[tk_tm, tk_tg])
        tk_pmt = gop(lambda e: e.partition_all_reduce(brm[:], brm[:], channels=128, reduce_op=ReduceOp.max), waits=[tk_tmx])
        tk_ivt = vop(lambda e: e.reciprocal(out=binv[:], in_=brm[:]), waits=[tk_pmt])
        tk_ivt2 = vop(lambda e: e.tensor_scalar(out=binv[:], in0=binv[:], scalar1=448.0, scalar2=None, op0=ALU.mult), waits=[tk_ivt])
        tk_ta = vop(lambda e: e.tensor_scalar(out=bscg[:], in0=brm[:], scalar1=float(np.float32(C448_HI) * np.float32(0.25)), scalar2=None, op0=ALU.mult), waits=[tk_pmt])
        tk_tb = vop(lambda e: e.tensor_scalar(out=btmp[:], in0=brm[:], scalar1=float(np.float32(C448_LO) * np.float32(0.25)), scalar2=None, op0=ALU.mult), waits=[tk_pmt])
        tk_tsc = vop(lambda e: e.tensor_tensor(out=bscg[:], in0=bscg[:], in1=btmp[:], op=ALU.add), waits=[tk_ta, tk_tb])

        tst_fin = [tstore[-2], tstore[-1]]
        qttok = []
        for j in range(J):
            tc = tch[j % 2]
            w = tst_fin + [tk_ivt2] + ([qttok[j - 2][1]] if j >= 2 else [])
            tk_lt = load(tc[:], tbuf[j], f"tchld{j % 2}", waits=w)
            tku = vop(lambda e, tcc=tc: e.tensor_tensor(out=uch[:], in0=tcc[:], in1=binv[:], op=ALU.mult), waits=[tk_lt])
            tkp = vop(lambda e: e._custom_dve(QP_ACT, out=pch[:], in0=uch[:], s0=CDN, s1=CLAMP), waits=[tku])
            tkq = vop(lambda e, jj=j: e._custom_dve(QR_ACT, out=qt[jj][:], in0=uch[:], in1=pch[:], s0=MAGIC, s1=CDN), waits=[tkp])
            qttok.append((qt[j], tkq))

        for o in range(OD):
            bank = py[o % 2]
            wts = [py_free[o % 2]] if py_free[o % 2] else []
            wld0 = [dn_half_tok[0]] if dn_half_tok[0] else []
            wld1 = [dn_half_tok[1]] if dn_half_tok[1] else []
            tk_ld0 = load(slabd[0][:].rearrange("p (c e) -> p c e", e=128), cd[o, :, 0:22, :], "slabd0", waits=wld0)
            tk_ld1 = load(slabd[1][:].rearrange("p (c e) -> p c e", e=128), cd[o, :, 22:44, :], "slabd1", waits=wld1)
            tk_m = None
            for j in range(J):
                w_mm = []
                if j == 0:
                    w_mm = [tk_ld0, qttok[-1][1]] + wts
                elif j == 22:
                    w_mm = [tk_ld1]
                tk_m = top(lambda e, jj=j, bk=bank: e.matmul(bk[:], slabd[jj // 22][:, (jj % 22) * 128:((jj % 22) + 1) * 128], qt[jj][:], start=(jj == 0), stop=(jj == J - 1)),
                           waits=w_mm)
                if j == 21:
                    dn_half_tok[0] = tk_m
            dn_half_tok[1] = tk_m
            tk_ysc = vop(lambda e, oo=o, bk=bank: e.scalar_tensor_tensor(out=btmp[:], in0=bk[:], scalar=wsd[:, oo:oo + 1], in1=bscg[:], op0=ALU.mult, op1=ALU.mult), waits=[tk_m, tk_tsc])
            py_free[o % 2] = tk_ysc
            tk_x2 = vop(lambda e, oo=o: e.tensor_tensor(out=xn2[oo][:], in0=btmp[:], in1=x1[oo][:], op=ALU.add), waits=[tk_ysc])
            carry_out_st[o] = store(outT[o * 128:(o + 1) * 128, t0:t0 + T], xn2[o][:], "out", waits=[tk_x2])
        carry_out_st = [carry_out_st[OD - 1]] * OD
        carry_T_last = S.now(tt)
        carry_T_last = (tt, carry_T_last[1])

    final_out = S.now(dma_sem("out"))
    S.add("gpsimd", lambda e: e.nop(), waits=[final_out])

    with nc.Block() as block:
        @block.sync
        def _(eng):
            S.replay("sync", eng)

        @block.vector
        def _(eng):
            S.replay("vector", eng)

        @block.scalar
        def _(eng):
            S.replay("scalar", eng)

        @block.tensor
        def _(eng):
            S.replay("tensor", eng)

        @block.gpsimd
        def _(eng):
            eng.add_instruction(bass_isa.InstPseudoReloadLibraryIndex(
                lib_index=4, name=nc.get_next_instruction_name()))
            S.replay("gpsimd", eng)

    for h in reversed(ctx):
        h.__exit__(None, None, None)

    mybir.codegen_inst_isa_subclasses(nc)
    return nc


_NC_CACHE = None


def kernel(L0_attn_x, L0_attn_gamma, L0_attn_beta, L0_attn_W_in, L0_attn_W_out,
           L0_attn_dt, L0_ffn_w_gate, L0_ffn_w_up, L0_ffn_w_down,
           L0_ffn_up_reorder_index, L0_ffn_down_reorder_index):
    global _NC_CACHE
    import ml_dtypes
    x = np.asarray(L0_attn_x)
    dni = np.asarray(L0_ffn_down_reorder_index).astype(np.int64)
    WinT = np.ascontiguousarray(np.asarray(L0_attn_W_in).T)
    WoutT = np.ascontiguousarray(np.asarray(L0_attn_W_out).T)
    wg2 = np.ascontiguousarray(np.asarray(L0_ffn_w_gate)[dni, :])
    wu2 = np.ascontiguousarray(np.asarray(L0_ffn_w_up)[dni, :])
    wdn = np.ascontiguousarray(np.asarray(L0_ffn_w_down))
    gamT = np.ascontiguousarray(np.asarray(L0_attn_gamma).reshape(16, 128).T)
    betT = np.ascontiguousarray(np.asarray(L0_attn_beta).reshape(16, 128).T)
    dtT = np.ascontiguousarray(np.asarray(L0_attn_dt).reshape(32, 128).T)
    ident = np.eye(128, dtype=ml_dtypes.bfloat16)

    if _NC_CACHE is None:
        _NC_CACHE = build_nc()
    nc = _NC_CACHE

    in_maps = []
    for c in range(8):
        in_maps.append({
            "xT": np.ascontiguousarray(x[c].T),
            "WinT": WinT, "WoutT": WoutT,
            "wg2": wg2, "wu2": wu2, "wdn": wdn,
            "gamT": gamT, "betT": betT, "dtT": dtT,
            "identD": ident,
        })
    res = run_bass_kernel_spmd(nc, in_maps, core_ids=list(range(8)))
    out = np.empty((8, 2048, 2048), dtype=np.float32)
    for c in range(8):
        out[c] = np.asarray(res.results[c]["outT"]).T
    return out


# version 09 (final)
# speedup vs baseline: 1.0733x; 1.0733x over previous
import numpy as np
import concourse.bass as bass
import concourse.mybir as mybir
from concourse.bass_utils import run_bass_kernel_spmd
import concourse.dve_ops as dve_ops
from concourse.dve_op import DveOp
from concourse.dve_spec import Spec, Src0, Src1, C0, C1, C2, MaxPos, maxx, lower, Bin, AluOp
from concourse.bass_isa import ReduceOp
from concourse import bass_isa

F32 = mybir.dt.float32
BF16 = mybir.dt.bfloat16
AF = mybir.ActivationFunctionType
ALU = mybir.AluOpType
AX = mybir.AxisListType

P = 128
T = 512
NT = 2048 // T
D = 16
EZ = 32
EO = 32
OD = 16
J = 44
EPS = 1e-5

MASKBITS = np.uint32(0x7F800000)
CDN = float(np.uint32(0x807FFFFF).view(np.float32))
MAGIC = 3145728.0
CLAMP = 0.015625
C448_HI = np.float32(1.0 / 448.0)
C448_LO = np.float32(np.float64(1.0) / 448.0 - np.float64(np.float32(1.0 / 448.0)))


def _np_exp_clamp(a):
    m_, e_ = np.frexp(a)
    e = (e_ - 1).astype(np.int64)
    e = np.where(a == 0, -6, e)
    return np.maximum(e, -6)


def _qp_act_ref(in0, in1, s0, s1, imm2):
    a = np.abs(np.asarray(in0))
    if a.dtype == np.float64:
        return np.ldexp(np.float64(1.0), 1 - _np_exp_clamp(a))
    b = np.ascontiguousarray(a, dtype=np.float32).view(np.uint32)
    t2 = (b & MASKBITS).view(np.float32)
    t3 = np.maximum(t2, np.float32(s1))
    return ((~np.ascontiguousarray(t3).view(np.uint32)) & MASKBITS).view(np.float32)


def _qr_act_ref(in0, in1, s0, s1, imm2):
    in0 = np.asarray(in0); in1 = np.asarray(in1)
    if in0.dtype == np.float64:
        return np.rint(in0 * in1 * 4.0) * 0.5 / in1
    m0 = (in0 * in1).astype(np.float32)
    m1 = (m0 + np.float32(s0)).astype(np.float32)
    r = (m1 - np.float32(s0)).astype(np.float32)
    p3 = ((~np.ascontiguousarray(in1, dtype=np.float32).view(np.uint32)) & MASKBITS).view(np.float32)
    return (r * p3).astype(np.float32)


def _qp_w_ref(in0, in1, s0, s1, imm2):
    in0 = np.asarray(in0)
    if in0.dtype == np.float64:
        u = np.abs(in0 * np.asarray(s0))
        return np.ldexp(np.float64(1.0), 1 - _np_exp_clamp(u))
    a = np.abs((in0 * np.asarray(s0, dtype=np.float32)).astype(np.float32))
    b = np.ascontiguousarray(a).view(np.uint32)
    t2 = (b & MASKBITS).view(np.float32)
    t3 = np.maximum(t2, np.float32(imm2))
    return ((~np.ascontiguousarray(t3).view(np.uint32)) & MASKBITS).view(np.float32)


def _qr_w_ref(in0, in1, s0, s1, imm2):
    in0 = np.asarray(in0); in1 = np.asarray(in1)
    if in0.dtype == np.float64:
        ws = np.asarray(s1) * (np.float64(1.0) / 448.0)
        u = in0 / ws
        return np.rint(u * in1 * 4.0) * 0.5 / in1
    u = (in0 * np.asarray(s0, dtype=np.float32)).astype(np.float32)
    m0 = (u * in1).astype(np.float32)
    m1 = (m0 + np.float32(imm2)).astype(np.float32)
    r = (m1 - np.float32(imm2)).astype(np.float32)
    p3 = ((~np.ascontiguousarray(in1, dtype=np.float32).view(np.uint32)) & MASKBITS).view(np.float32)
    return (r * p3).astype(np.float32)


_REPLACE_SLOTS = ["SHIFT_RIGHT_TT_BV", "DVEHASH32L_SEED", "DVEHASH32L_BODY",
                  "DVEHASH32L_DRAIN"]


def _mk_op(name, body, ref):
    for op in dve_ops.OPS:
        if op.name == name:
            return op
    spec = Spec(body=body)
    shas = {}
    for ver in ("v3", "v4"):
        try:
            shas[ver] = lower(spec, ver=ver, perf_en=False, name=name).sha(ver)
        except Exception:
            pass
    op = DveOp(name, spec, subdim=False, uops_sha=shas, reference=ref)
    victim = _REPLACE_SLOTS.pop(0)
    idx = next(i for i, o in enumerate(dve_ops.OPS) if o.name == victim)
    dve_ops.OPS[idx] = op
    dve_ops._SUB_OPCODE_FOR_NAME[name] = dve_ops._CUSTOM_DVE_ROW_BASE + idx
    dve_ops.CUSTOM_DVE_REFS[name] = op.ref
    dve_ops.CUSTOM_DVE_SPECS[name] = spec
    return op


def _register_ops():
    mask = Bin(AluOp.BITWISE_NOT, C0, C0)
    t2 = Bin(AluOp.BITWISE_AND, Src0, mask)
    t3 = maxx(t2, C1)
    n1 = Bin(AluOp.BITWISE_NOT, t3, t3)
    qp_act = _mk_op("ANT_QPA", Bin(AluOp.BITWISE_AND, n1, mask), _qp_act_ref)

    mask2 = Bin(AluOp.BITWISE_NOT, C1, C1)
    m0 = Src0 * Src1
    m1 = m0 + C0
    r = m1 - C0
    pn = Bin(AluOp.BITWISE_NOT, Src1, Src1)
    p3 = Bin(AluOp.BITWISE_AND, pn, mask2)
    qr_act = _mk_op("ANT_QRA", r * p3, _qr_act_ref)

    mask3 = Bin(AluOp.BITWISE_NOT, C1, C1)
    tw = Src0 * C0
    t2w = Bin(AluOp.BITWISE_AND, tw, mask3)
    t3w = maxx(t2w, C2)
    n1w = Bin(AluOp.BITWISE_NOT, t3w, t3w)
    qp_w = _mk_op("ANT_QPW", Bin(AluOp.BITWISE_AND, n1w, mask3), _qp_w_ref)

    maskw = Bin(AluOp.ADD, MaxPos, MaxPos)
    tw2 = Src0 * C0
    m0w = tw2 * Src1
    m1w = m0w + C2
    rw = m1w - C2
    pnw = Bin(AluOp.BITWISE_NOT, Src1, Src1)
    p3w = Bin(AluOp.BITWISE_AND, pnw, maskw)
    qr_w = _mk_op("ANT_QRW", rw * p3w, _qr_w_ref)
    return qp_act, qr_act, qp_w, qr_w


QP_ACT, QR_ACT, QP_W, QR_W = _register_ops()


class Sched:

    def __init__(self):
        self.steps = []
        self.counts = {}

    def add(self, engine, emit, waits=(), inc=None):
        w = [(s, v) for (s, v) in waits if s is not None and v > 0]
        tok = None
        if inc is not None:
            sem, amt = inc
            key = id(sem)
            self.counts[key] = self.counts.get(key, 0) + amt
            tok = (sem, self.counts[key])
        self.steps.append((engine, tuple(w), emit, inc))
        return tok

    def now(self, sem):
        return (sem, self.counts.get(id(sem), 0))

    def replay(self, engine, eng):
        last_wait = {}
        for (e, waits, emit, inc) in self.steps:
            if e != engine:
                continue
            for (sem, val) in waits:
                if last_wait.get(id(sem), -1) < val:
                    eng.wait_ge(sem, val)
                    last_wait[id(sem)] = val
            inst = emit(eng)
            if inc is not None:
                sem, amt = inc
                inst.then_inc(sem, amt)


_LAST_S = None


def build_nc():
    global _LAST_S
    nc = bass.Bass("TRN2", target_bir_lowering=False)

    xT = nc.dram_tensor("xT", [2048, 2048], F32, kind="ExternalInput").ap()
    WinT = nc.dram_tensor("WinT", [2048, 8192], F32, kind="ExternalInput").ap()
    WoutT = nc.dram_tensor("WoutT", [4096, 2048], F32, kind="ExternalInput").ap()
    wg2 = nc.dram_tensor("wg2", [5632, 2048], F32, kind="ExternalInput").ap()
    wu2 = nc.dram_tensor("wu2", [5632, 2048], F32, kind="ExternalInput").ap()
    wdn = nc.dram_tensor("wdn", [2048, 5632], F32, kind="ExternalInput").ap()
    gamT = nc.dram_tensor("gamT", [128, 16], F32, kind="ExternalInput").ap()
    betT = nc.dram_tensor("betT", [128, 16], F32, kind="ExternalInput").ap()
    dtT = nc.dram_tensor("dtT", [128, 32], F32, kind="ExternalInput").ap()
    identD = nc.dram_tensor("identD", [128, 128], BF16, kind="ExternalInput").ap()
    outT = nc.dram_tensor("outT", [2048, 2048], F32, kind="ExternalOutput").ap()

    cWin = nc.dram_tensor("cWin", [64, 128, D, 128], BF16, kind="Internal").ap()
    cWout = nc.dram_tensor("cWout", [OD, 128, EO, 128], BF16, kind="Internal").ap()
    cg = nc.dram_tensor("cg", [J, 128, D, 128], BF16, kind="Internal").ap()
    cu = nc.dram_tensor("cu", [J, 128, D, 128], BF16, kind="Internal").ap()
    cd = nc.dram_tensor("cd", [OD, 128, J, 128], BF16, kind="Internal").ap()
    tbuf = nc.dram_tensor("tbuf", [J, 128, T], F32, kind="Internal").ap()

    S = Sched()
    _LAST_S = S
    ctx = []

    def sb(name, shape, dtype):
        h = nc.sbuf_tensor(name, shape, dtype)
        t = h.__enter__()
        ctx.append(h)
        return t

    def ps(name, shape, dtype):
        h = nc.psum_tensor(name, shape, dtype)
        t = h.__enter__()
        ctx.append(h)
        return t

    ident = sb("ident", [P, 128], BF16)
    gam = sb("gam", [P, D], F32)
    ngam = sb("ngam", [P, D], F32)
    bet = sb("bet", [P, D], F32)
    dtt = sb("dtt", [P, 32], F32)
    dtw = sb("dtw", [P, 32], F32)
    wsg = sb("wsg", [P, J], F32)
    wsu = sb("wsu", [P, J], F32)
    wsd = sb("wsd", [P, OD], F32)

    vmax = sb("vmax", [P, 1], F32)
    vtmp = sb("vtmp", [P, 1], F32)
    vws = sb("vws", [P, 1], F32)
    vinv = sb("vinv", [P, 1], F32)

    pctx = []

    def sbp(name, shape, dtype):
        h = nc.sbuf_tensor(name, shape, dtype)
        t = h.__enter__()
        pctx.append(h)
        return t

    wband = [sbp(f"wband{i}", [P, 4096], F32) for i in range(2)]
    wband2 = [sbp(f"wband2{i}", [P, 5632], F32) for i in range(2)]
    bfband = sbp("bfband", [P, 8192], BF16)
    qband = sbp("qband", [P, 5632], BF16)
    pband = sbp("pband", [P, 5632], F32)
    trev = [sbp(f"trev{i}", [P, 128], BF16) for i in range(2)]

    pz = [ps("pz0", [P, T], F32)]
    pg = [ps("pg0", [P, T], F32)]
    py = [ps(f"py{i}", [P, T], F32) for i in range(2)]
    pgu = [ps(f"pgu{i}", [P, T], F32) for i in range(2)]
    ptr_h = [nc.psum_tensor(f"ptr{i}", [P, 128], BF16) for i in range(2)]
    ptr = [h.__enter__() for h in ptr_h]

    sems = {}
    for name in ["sv", "ss", "tt", "sg"]:
        h = nc.semaphore(name)
        sems[name] = h.__enter__()
        ctx.append(h)
    sv, ss, tt, sg = (sems[k] for k in ["sv", "ss", "tt", "sg"])

    dma_sems = {}

    def dma_sem(key):
        if key not in dma_sems:
            h = nc.semaphore(f"q_{key}")
            dma_sems[key] = h.__enter__()
            ctx.append(h)
        return dma_sems[key]

    def load(out, in_, key, waits=()):
        return S.add("sync", (lambda o, i: (lambda e: e.dma_start(out=o, in_=i)))(out, in_),
                     waits=waits, inc=(dma_sem(key), 16))

    def store(out, in_, key, waits=()):
        return S.add("scalar", (lambda o, i: (lambda e: e.dma_start(out=o, in_=i)))(out, in_),
                     waits=waits, inc=(dma_sem(key), 16))

    def _chain(eng_sem, waits):
        w = list(waits)
        if S.counts.get(id(eng_sem), 0) > 0:
            w.append(S.now(eng_sem))
        return w

    def vop(emit, waits=()):
        return S.add("vector", emit, waits=_chain(sv, waits), inc=(sv, 1))

    def sop(emit, waits=()):
        return S.add("scalar", emit, waits=_chain(ss, waits), inc=(ss, 1))

    def top(emit, waits=()):
        return S.add("tensor", emit, waits=waits, inc=(tt, 1))

    def gop(emit, waits=()):
        return S.add("gpsimd", emit, waits=_chain(sg, waits), inc=(sg, 1))

    tk_ident = load(ident[:], identD, "ident")
    tk_gam = load(gam[:], gamT, "gam")
    tk_bet = load(bet[:], betT, "bet")
    tk_dt = load(dtt[:], dtT, "dt")
    tk_ngam = vop(lambda e: e.tensor_scalar(out=ngam[:], in0=gam[:], scalar1=-1.0, scalar2=None, op0=ALU.mult), waits=[tk_gam])

    t1 = vop(lambda e: e.tensor_scalar(out=dtw[:], in0=dtt[:], scalar1=-1.0, scalar2=None, op0=ALU.mult), waits=[tk_dt])
    t2 = vop(lambda e: e.tensor_reduce(out=vmax[:], in_=dtw[:], axis=AX.X, op=ALU.max), waits=[t1])
    t3 = gop(lambda e: e.partition_all_reduce(vtmp[:], vmax[:], channels=128, reduce_op=ReduceOp.max), waits=[t2])
    t4 = vop(lambda e: e.tensor_scalar(out=vtmp[:], in0=vtmp[:], scalar1=-1.0, scalar2=None, op0=ALU.mult), waits=[t3])
    t5 = sop(lambda e: e.activation(out=dtw[:], in_=dtw[:], func=AF.Exp, bias=vtmp[:, 0:1]), waits=[t4])
    t6 = vop(lambda e: e.tensor_reduce(out=vmax[:], in_=dtw[:], axis=AX.X, op=ALU.add), waits=[t5])
    t7 = gop(lambda e: e.partition_all_reduce(vtmp[:], vmax[:], channels=128, reduce_op=ReduceOp.add), waits=[t6])
    t8 = vop(lambda e: e.reciprocal(out=vtmp[:], in_=vtmp[:]), waits=[t7])
    tk_dtw = vop(lambda e: e.tensor_scalar(out=dtw[:], in0=dtw[:], scalar1=vtmp[:, 0:1], scalar2=None, op0=ALU.mult), waits=[t8])

    prev_cast = {}
    prev_store = None
    for c in range(D):
        for h in range(2):
            k = (c * 2 + h) % 2
            buf = wband[k]
            w = [prev_cast.get(k)] if prev_cast.get(k) else []
            tk_l = load(buf[:], WinT[c * 128:(c + 1) * 128, h * 4096:(h + 1) * 4096], f"wband{k}", waits=w)
            wcast = [tk_l] + ([prev_store] if h == 0 and prev_store else [])
            tk_c = vop(lambda e, b=buf, hh=h: e.tensor_copy(out=bfband[:, hh * 4096:(hh + 1) * 4096], in_=b[:]), waits=wcast)
            prev_cast[k] = tk_c
        tk_c2 = prev_cast[(c * 2 + 1) % 2]
        for E in range(64):
            prev_store = store(cWin[E, :, c, :], bfband[:, E * 128:(E + 1) * 128], "bfband", waits=[tk_c2])

    for c in range(EO):
        k = c % 2
        buf = wband[k]
        w = [prev_cast.get(k)] if prev_cast.get(k) else []
        tk_l = load(buf[:, 0:2048], WoutT[c * 128:(c + 1) * 128, :], f"wband{k}", waits=w)
        tk_c = vop(lambda e, b=buf: e.tensor_copy(out=bfband[:, 0:2048], in_=b[:, 0:2048]), waits=[tk_l, prev_store] if prev_store else [tk_l])
        prev_cast[k] = tk_c
        for O in range(OD):
            prev_store = store(cWout[O, :, c, :], bfband[:, O * 128:(O + 1) * 128], "bfband", waits=[tk_c])

    last_tr = {0: None, 1: None}
    last_ev = {0: None, 1: None}
    last_evst = {0: None, 1: None}
    prev_q = None
    prev_ld2 = {0: None, 1: None}

    def quant_weight_matrix(src, nbands, width, ws_cols, cache):
        nonlocal prev_q
        nin = width // 128
        for b in range(nbands):
            k = b % 2
            buf = wband2[k]
            w = [prev_ld2[k]] if prev_ld2[k] else []
            tk_l = load(buf[:, 0:width], src[b * 128:(b + 1) * 128, :], f"wband2{k}", waits=w)
            tk_m = vop(lambda e, bb=buf, wd=width: e.tensor_reduce(out=vmax[:], in_=bb[:, 0:wd], axis=AX.X, op=ALU.max, apply_absolute_value=True), waits=[tk_l])
            tk_a = vop(lambda e: e.tensor_scalar(out=vws[:], in0=vmax[:], scalar1=float(C448_HI), scalar2=None, op0=ALU.mult), waits=[tk_m])
            tk_b = vop(lambda e: e.tensor_scalar(out=vtmp[:], in0=vmax[:], scalar1=float(C448_LO), scalar2=None, op0=ALU.mult), waits=[tk_m])
            tk_w = vop(lambda e: e.tensor_tensor(out=vws[:], in0=vws[:], in1=vtmp[:], op=ALU.add), waits=[tk_a, tk_b])
            vop(lambda e, col=ws_cols[b]: e.tensor_copy(out=col, in_=vws[:]), waits=[tk_w])
            tk_i = vop(lambda e: e.reciprocal(out=vinv[:], in_=vws[:]), waits=[tk_w])
            wq = [tk_i] + [x for x in (last_tr[0], last_tr[1]) if x]
            tk_p = vop(lambda e, bb=buf, wd=width: e._custom_dve(QP_W, out=pband[:, 0:wd], in0=bb[:, 0:wd], s0=vinv[:, 0:1], s1=CDN, imm2=CLAMP), waits=wq)
            tk_q = vop(lambda e, bb=buf, wd=width: e._custom_dve(QR_W, out=qband[:, 0:wd], in0=bb[:, 0:wd], in1=pband[:, 0:wd], s0=vinv[:, 0:1], s1=vmax[:, 0:1], imm2=MAGIC), waits=[tk_p])
            prev_q = tk_q
            prev_ld2[k] = tk_q
            for c in range(nin):
                bk = c % 2
                wtr = [tk_q, tk_ident] + ([last_ev[bk]] if last_ev[bk] else [])
                tk_t = top(lambda e, cc=c, bank=ptr[bk]: e.transpose(out=bank[:], in_=qband[:, cc * 128:(cc + 1) * 128], identity=ident[:]), waits=wtr)
                last_tr[bk] = tk_t
                wev = [tk_t] + ([last_evst[bk]] if last_evst[bk] else [])
                tk_e = vop(lambda e, bank=ptr[bk], evv=trev[bk]: e.tensor_copy(out=evv[:], in_=bank[:]), waits=wev)
                last_ev[bk] = tk_e
                last_evst[bk] = store(cache[b, :, c, :], trev[bk][:], f"trev{bk}", waits=[tk_e])

    wsg_cols = [wsg[:, j:j + 1] for j in range(J)]
    wsu_cols = [wsu[:, j:j + 1] for j in range(J)]
    wsd_cols = [wsd[:, o:o + 1] for o in range(OD)]
    quant_weight_matrix(wg2, J, 2048, wsg_cols, cg)
    quant_weight_matrix(wu2, J, 2048, wsu_cols, cu)
    quant_weight_matrix(wdn, OD, 5632, wsd_cols, cd)

    prolog_done = [S.now(sv), S.now(tt), S.now(ss), S.now(sg),
                   S.now(dma_sem("bfband")), S.now(dma_sem("trev0")),
                   S.now(dma_sem("trev1"))]

    for h in reversed(pctx):
        h.__exit__(None, None, None)
    for h in reversed(ptr_h):
        h.__exit__(None, None, None)
    pz.append(ps("pz1", [P, T], F32))
    pg.append(ps("pg1", [P, T], F32))

    slabz = [sb(f"slabz{i}", [P, D * 128], BF16) for i in range(2)]
    slabg = [sb(f"slabg{i}", [P, D * 128], BF16) for i in range(2)]
    slabo = [sb(f"slabo{i}", [P, 2048], BF16) for i in range(2)]
    slabgu = [sb(f"slabgu{i}", [P, D * 128], BF16) for i in range(4)]
    slabd = [sb(f"slabd{i}", [P, 22 * 128], BF16) for i in range(2)]

    xa = [sb(f"xa{c}", [P, T], F32) for c in range(D)]
    xn0 = [sb(f"xn0{c}", [P, T], F32) for c in range(D)]
    xnb = [sb(f"xnb{c}", [P, T], BF16) for c in range(D)]
    qt = [sb(f"qt{j}", [P, T], BF16) for j in range(J)]
    outp = qt[:EO]
    x1 = xn0
    xn2 = xa
    qx = xnb
    tch = [sb(f"tch{i}", [P, T], F32) for i in range(2)]
    uch = sb("uch", [P, T], F32)
    pch = sb("pch", [P, T], F32)

    bsum = sb("bsum", [P, T], F32)
    bsq = sb("bsq", [P, T], F32)
    sqt = sb("sqt", [P, T], F32)
    btmp2 = sqt
    bmu = sb("bmu", [P, T], F32)
    brstd = sb("brstd", [P, T], F32)
    bmurstd = sb("bmurstd", [P, T], F32)
    bmu2, brstd2, bm2r2 = bmu, brstd, bmurstd
    brm = sb("brm", [P, T], F32)
    binv = sb("binv", [P, T], F32)
    bscg = sb("bscg", [P, T], F32)
    btmp = sb("btmp", [P, T], F32)
    bvarc = bscg
    zcb = bsum
    sgb = bsq
    ztb = brm
    gcb = binv
    gsb = gcb
    gscb = bmu
    uscb = bmurstd

    def ln_stats(src_chunks, mu_t, rstd_t, murstd_t, varc_out=None):
        toks = []
        tk_lastq = None
        for c in range(D):
            tk_sq = sop(lambda e, cc=c: e.activation(out=sqt[:], in_=src_chunks[cc][0][:], func=AF.Square),
                        waits=[src_chunks[c][1]] + ([tk_lastq] if tk_lastq else []))
            if c == 0:
                tk_s = vop(lambda e, cc=c: e.tensor_copy(out=bsum[:], in_=src_chunks[cc][0][:]), waits=[src_chunks[c][1]])
                tk_qq = vop(lambda e: e.tensor_copy(out=bsq[:], in_=sqt[:]), waits=[tk_sq])
            else:
                tk_s = vop(lambda e, cc=c: e.tensor_tensor(out=bsum[:], in0=bsum[:], in1=src_chunks[cc][0][:], op=ALU.add), waits=[src_chunks[c][1]])
                tk_qq = vop(lambda e: e.tensor_tensor(out=bsq[:], in0=bsq[:], in1=sqt[:], op=ALU.add), waits=[tk_sq])
            tk_lastq = tk_qq
            toks.append((tk_s, tk_qq))
        tk_ps = gop(lambda e: e.partition_all_reduce(bsum[:], bsum[:], channels=128, reduce_op=ReduceOp.add), waits=[toks[-1][0]])
        tk_pq = gop(lambda e: e.partition_all_reduce(bsq[:], bsq[:], channels=128, reduce_op=ReduceOp.add), waits=[toks[-1][1]])
        tk_mu = vop(lambda e: e.tensor_scalar(out=mu_t[:], in0=bsum[:], scalar1=float(np.float32(1.0 / 2048.0)), scalar2=None, op0=ALU.mult), waits=[tk_ps])
        tk_ms = vop(lambda e: e.tensor_scalar(out=bsq[:], in0=bsq[:], scalar1=float(np.float32(1.0 / 2048.0)), scalar2=None, op0=ALU.mult), waits=[tk_pq])
        tk_m2 = vop(lambda e: e.tensor_tensor(out=btmp2[:], in0=mu_t[:], in1=mu_t[:], op=ALU.mult), waits=[tk_mu])
        tk_var = vop(lambda e: e.tensor_tensor(out=btmp2[:], in0=bsq[:], in1=btmp2[:], op=ALU.subtract), waits=[tk_ms, tk_m2])
        tk_vc = None
        if varc_out is not None:
            tk_vc = vop(lambda e: e.tensor_scalar(out=varc_out[:], in0=btmp2[:],
                                                  scalar1=float(np.float32(1.0) + np.float32(EPS)),
                                                  scalar2=float(np.float32(EPS) * np.float32(EPS)),
                                                  op0=ALU.mult, op1=ALU.add), waits=[tk_var])
        tk_ln = sop(lambda e: e.activation(out=btmp2[:], in_=btmp2[:], func=AF.Ln, bias=np.float32(EPS)), waits=[tk_var] + ([tk_vc] if tk_vc else []))
        tk_rs = sop(lambda e: e.activation(out=rstd_t[:], in_=btmp2[:], func=AF.Exp, scale=np.float32(-0.5)), waits=[tk_ln])
        tk_mr = vop(lambda e: e.tensor_tensor(out=murstd_t[:], in0=mu_t[:], in1=rstd_t[:], op=ALU.mult), waits=[tk_rs])
        return tk_rs, tk_mr, tk_vc

    carry_T_last = None
    carry_out_st = [None] * OD
    carry_x1_ev = [None] * D
    z_mm_last = {0: None, 1: None}
    g_mm_last = {0: None, 1: None}
    pz_evac = [None, None]
    pg_evac = [None, None]
    prev_gu_evac = None
    py_free = [None, None]
    mm2_half_tok = [None, None]
    dn_half_tok = [None, None]

    for tile in range(NT):
        t0 = tile * T
        xld = None
        for c in range(D):
            w = ([carry_T_last] if carry_T_last else []) +                 ([carry_x1_ev[c]] if carry_x1_ev[c] else []) +                 ([carry_out_st[c]] if carry_out_st[c] else []) +                 (prolog_done if tile == 0 else [])
            xld = load(xa[c][:], xT[c * 128:(c + 1) * 128, t0:t0 + T], "xa", waits=w)
        xtok = [(xa[c], xld) for c in range(D)]

        w0 = ([carry_T_last] if carry_T_last else []) + [x for x in carry_out_st if x] +              (prolog_done if tile == 0 else [])
        if w0:
            vop(lambda e: e.nop(), waits=w0)

        tk_rs1, tk_mr1, tk_vc1 = ln_stats(xtok, bmu, brstd, bmurstd, varc_out=bvarc)
        tk_lnc = sop(lambda e: e.activation(out=bvarc[:], in_=bvarc[:], func=AF.Ln), waits=[tk_vc1])
        tk_sub = vop(lambda e: e.tensor_tensor(out=bvarc[:], in0=btmp2[:], in1=bvarc[:], op=ALU.subtract), waits=[tk_lnc])
        tk_ratio = sop(lambda e: e.activation(out=bvarc[:], in_=bvarc[:], func=AF.Exp, scale=np.float32(0.5)), waits=[tk_sub])
        xn0tok = []
        xnbtok = []
        for c in range(D):
            tk1 = vop(lambda e, cc=c: e.tensor_tensor(out=xn0[cc][:], in0=xa[cc][:], in1=brstd[:], op=ALU.mult), waits=[xtok[c][1], tk_rs1])
            tk2 = vop(lambda e, cc=c: e.tensor_tensor(out=xn0[cc][:], in0=xn0[cc][:], in1=bmurstd[:], op=ALU.subtract), waits=[tk_mr1])
            xn0tok.append((xn0[c], tk2))
            tka = vop(lambda e, cc=c: e.scalar_tensor_tensor(out=btmp[:], in0=xn0[cc][:], scalar=gam[:, cc:cc + 1], in1=bvarc[:], op0=ALU.mult, op1=ALU.mult), waits=[tk_ratio, tk_gam])
            tkb = vop(lambda e, cc=c: e.tensor_scalar(out=xnb[cc][:], in0=btmp[:], scalar1=bet[:, cc:cc + 1], scalar2=None, op0=ALU.add), waits=[tk_bet])
            xnbtok.append((xnb[c], tkb))
        xstok = []
        for c in range(D):
            tk = vop(lambda e, cc=c: e.tensor_tensor(out=xa[cc][:], in0=xa[cc][:], in1=xn0[cc][:], op=ALU.add), waits=[xnbtok[c][1]])
            xstok.append((xa[c], tk))

        outptok = []
        for ep in range(EZ):
            k = ep % 2
            bz = slabz[k]
            bg = slabg[k]
            tk_lz = load(bz[:].rearrange("p (c e) -> p c e", e=128), cWin[ep], f"slabz{k}",
                         waits=[z_mm_last[k]] if z_mm_last[k] else [])
            tk_lg = load(bg[:].rearrange("p (c e) -> p c e", e=128), cWin[ep + 32], f"slabg{k}",
                         waits=[g_mm_last[k]] if g_mm_last[k] else [])
            tk_mz = None
            for c in range(D):
                w_mm = []
                if c == 0:
                    w_mm = [tk_lz] + ([pz_evac[k]] if pz_evac[k] else [])
                if ep == 0:
                    w_mm = w_mm + [xnbtok[c][1]]
                tk_mz = top(lambda e, cc=c, bb=bz, kk=k: e.matmul(pz[kk][:], bb[:, cc * 128:(cc + 1) * 128], xnb[cc][:], start=(cc == 0), stop=(cc == D - 1)),
                            waits=w_mm)
            z_mm_last[k] = tk_mz
            tk_mg = None
            for c in range(D):
                tk_mg = top(lambda e, cc=c, bb=bg, kk=k: e.matmul(pg[kk][:], bb[:, cc * 128:(cc + 1) * 128], xnb[cc][:], start=(cc == 0), stop=(cc == D - 1)),
                            waits=([tk_lg] + ([pg_evac[k]] if pg_evac[k] else []) if c == 0 else []))
            g_mm_last[k] = tk_mg
            tk_zc = vop(lambda e, kk=k: e.tensor_scalar(out=zcb[:], in0=pz[kk][:], scalar1=15.0, scalar2=-15.0, op0=ALU.min, op1=ALU.max), waits=[tk_mz])
            tk_sg = sop(lambda e: e.activation(out=sgb[:], in_=zcb[:], func=AF.Sigmoid), waits=[tk_zc])
            tk_zt = vop(lambda e, kk=k: e.tensor_tensor(out=ztb[:], in0=pz[kk][:], in1=sgb[:], op=ALU.mult), waits=[tk_sg])
            pz_evac[k] = tk_zt
            tk_gc = vop(lambda e, kk=k: e.tensor_scalar(out=gcb[:], in0=pg[kk][:], scalar1=15.0, scalar2=-15.0, op0=ALU.min, op1=ALU.max), waits=[tk_mg])
            pg_evac[k] = tk_gc
            tk_gs = sop(lambda e: e.activation(out=gsb[:], in_=gcb[:], func=AF.Silu), waits=[tk_gc])
            tk_o = vop(lambda e, eep=ep: e.scalar_tensor_tensor(out=outp[eep][:], in0=ztb[:], scalar=dtw[:, eep:eep + 1], in1=gsb[:], op0=ALU.mult, op1=ALU.mult), waits=[tk_gs, tk_dtw])
            outptok.append((outp[ep], tk_o))

        x1tok = []
        for o in range(OD):
            bank = py[o % 2]
            wts = [py_free[o % 2]] if py_free[o % 2] else []
            wlo0 = [mm2_half_tok[0]] if mm2_half_tok[0] else []
            wlo1 = [mm2_half_tok[1]] if mm2_half_tok[1] else []
            tk_lo0 = load(slabo[0][:].rearrange("p (c e) -> p c e", e=128), cWout[o, :, 0:16, :], "slabo0", waits=wlo0)
            tk_lo1 = load(slabo[1][:].rearrange("p (c e) -> p c e", e=128), cWout[o, :, 16:32, :], "slabo1", waits=wlo1)
            tk_m = None
            for c in range(EO):
                w_mm = []
                if c == 0:
                    w_mm = [tk_lo0, outptok[-1][1]] + wts
                elif c == 16:
                    w_mm = [tk_lo1]
                tk_m = top(lambda e, cc=c, bk=bank: e.matmul(bk[:], slabo[cc // 16][:, (cc % 16) * 128:((cc % 16) + 1) * 128], outp[cc][:], start=(cc == 0), stop=(cc == EO - 1)),
                           waits=w_mm)
                if c == 15:
                    mm2_half_tok[0] = tk_m
            mm2_half_tok[1] = tk_m
            tk_x1 = vop(lambda e, oo=o, bk=bank: e.tensor_tensor(out=x1[oo][:], in0=bk[:], in1=xa[oo][:], op=ALU.add), waits=[tk_m, xstok[o][1]])
            py_free[o % 2] = tk_x1
            x1tok.append((x1[o], tk_x1))
            carry_x1_ev[o] = tk_x1

        tk_rs3, tk_mr3, _ = ln_stats(x1tok, bmu, brstd, bmurstd)
        xn2tok = []
        for c in range(D):
            w = [x1tok[c][1], tk_rs3] + ([carry_out_st[c]] if carry_out_st[c] else [])
            tk1 = vop(lambda e, cc=c: e.tensor_tensor(out=xn2[cc][:], in0=x1[cc][:], in1=brstd[:], op=ALU.mult), waits=w)
            tk2 = vop(lambda e, cc=c: e.tensor_tensor(out=xn2[cc][:], in0=xn2[cc][:], in1=bmurstd[:], op=ALU.subtract), waits=[tk_mr3])
            xn2tok.append((xn2[c], tk2))
        tk_cm = None
        tk_cn = None
        for c in range(D):
            if c == 0:
                tk_cm = vop(lambda e: e.tensor_copy(out=brm[:], in_=xn2[0][:]), waits=[xn2tok[0][1]])
                tk_cn = vop(lambda e: e.tensor_copy(out=bsum[:], in_=xn2[0][:]))
            else:
                tk_cm = vop(lambda e, cc=c: e.tensor_tensor(out=brm[:], in0=brm[:], in1=xn2[cc][:], op=ALU.max), waits=[xn2tok[c][1]])
                tk_cn = vop(lambda e, cc=c: e.tensor_tensor(out=bsum[:], in0=bsum[:], in1=xn2[cc][:], op=ALU.min))
        tk_cg = vop(lambda e: e.tensor_scalar(out=bsum[:], in0=bsum[:], scalar1=-1.0, scalar2=None, op0=ALU.mult), waits=[tk_cn])
        tk_cm = vop(lambda e: e.tensor_tensor(out=brm[:], in0=brm[:], in1=bsum[:], op=ALU.max), waits=[tk_cm, tk_cg])
        tk_pm = gop(lambda e: e.partition_all_reduce(brm[:], brm[:], channels=128, reduce_op=ReduceOp.max), waits=[tk_cm])
        tk_iv = vop(lambda e: e.reciprocal(out=binv[:], in_=brm[:]), waits=[tk_pm])
        tk_iv2 = vop(lambda e: e.tensor_scalar(out=binv[:], in0=binv[:], scalar1=448.0, scalar2=None, op0=ALU.mult), waits=[tk_iv])
        tk_sa = vop(lambda e: e.tensor_scalar(out=bscg[:], in0=brm[:], scalar1=float(np.float32(C448_HI) * np.float32(0.25)), scalar2=None, op0=ALU.mult), waits=[tk_pm])
        tk_sb = vop(lambda e: e.tensor_scalar(out=btmp[:], in0=brm[:], scalar1=float(np.float32(C448_LO) * np.float32(0.25)), scalar2=None, op0=ALU.mult), waits=[tk_pm])
        tk_sc = vop(lambda e: e.tensor_tensor(out=bscg[:], in0=bscg[:], in1=btmp[:], op=ALU.add), waits=[tk_sa, tk_sb])
        qxtok = []
        for c in range(D):
            tku = vop(lambda e, cc=c: e.tensor_tensor(out=uch[:], in0=xn2[cc][:], in1=binv[:], op=ALU.mult), waits=[tk_iv2])
            tkp = vop(lambda e: e._custom_dve(QP_ACT, out=pch[:], in0=uch[:], s0=CDN, s1=CLAMP), waits=[tku])
            tkq = vop(lambda e, cc=c: e._custom_dve(QR_ACT, out=qx[cc][:], in0=uch[:], in1=pch[:], s0=MAGIC, s1=CDN), waits=[tkp])
            qxtok.append((qx[c], tkq))

        tk_tm = None
        tstore = []
        for j in range(J):
            bgu = slabgu[(j % 2) * 2]
            buu = slabgu[(j % 2) * 2 + 1]
            wts = [prev_gu_evac] if prev_gu_evac else []
            tk_lg2 = load(bgu[:].rearrange("p (c e) -> p c e", e=128), cg[j], f"slabgu{(j % 2) * 2}", waits=wts)
            tk_lu2 = load(buu[:].rearrange("p (c e) -> p c e", e=128), cu[j], f"slabgu{(j % 2) * 2 + 1}", waits=wts)
            tk_mg2 = None
            for c in range(D):
                tk_mg2 = top(lambda e, cc=c, bb=bgu: e.matmul(pgu[0][:], bb[:, cc * 128:(cc + 1) * 128], qx[cc][:], start=(cc == 0), stop=(cc == D - 1)),
                             waits=([tk_lg2, qxtok[-1][1]] + wts if c == 0 else []))
            tk_mu2 = None
            for c in range(D):
                tk_mu2 = top(lambda e, cc=c, bb=buu: e.matmul(pgu[1][:], bb[:, cc * 128:(cc + 1) * 128], qx[cc][:], start=(cc == 0), stop=(cc == D - 1)),
                             waits=([tk_lu2] + wts if c == 0 else []))
            tk_gsc = vop(lambda e, jj=j: e.scalar_tensor_tensor(out=gscb[:], in0=pgu[0][:], scalar=wsg[:, jj:jj + 1], in1=bscg[:], op0=ALU.mult, op1=ALU.mult), waits=[tk_mg2, tk_sc])
            tk_usc = vop(lambda e, jj=j: e.scalar_tensor_tensor(out=uscb[:], in0=pgu[1][:], scalar=wsu[:, jj:jj + 1], in1=bscg[:], op0=ALU.mult, op1=ALU.mult), waits=[tk_mu2])
            prev_gu_evac = tk_usc
            tk_gsl = sop(lambda e: e.activation(out=gscb[:], in_=gscb[:], func=AF.Silu), waits=[tk_gsc])
            tc = tch[j % 2]
            tk_t = vop(lambda e, tcc=tc: e.tensor_tensor(out=tcc[:], in0=gscb[:], in1=uscb[:], op=ALU.mult),
                       waits=[tk_gsl] + ([tstore[j - 2]] if j >= 2 else []))
            if j == 0:
                tk_tm = vop(lambda e, tcc=tc: e.tensor_copy(out=brm[:], in_=tcc[:]), waits=[tk_t])
                tk_tn = vop(lambda e, tcc=tc: e.tensor_copy(out=bsum[:], in_=tcc[:]))
            else:
                tk_tm = vop(lambda e, tcc=tc: e.tensor_tensor(out=brm[:], in0=brm[:], in1=tcc[:], op=ALU.max), waits=[tk_t])
                tk_tn = vop(lambda e, tcc=tc: e.tensor_tensor(out=bsum[:], in0=bsum[:], in1=tcc[:], op=ALU.min))
            tstore.append(store(tbuf[j], tc[:], f"tchst{j % 2}", waits=[tk_tn]))

        tk_tg = vop(lambda e: e.tensor_scalar(out=bsum[:], in0=bsum[:], scalar1=-1.0, scalar2=None, op0=ALU.mult), waits=[tk_tn])
        tk_tmx = vop(lambda e: e.tensor_tensor(out=brm[:], in0=brm[:], in1=bsum[:], op=ALU.max), waits=[tk_tm, tk_tg])
        tk_pmt = gop(lambda e: e.partition_all_reduce(brm[:], brm[:], channels=128, reduce_op=ReduceOp.max), waits=[tk_tmx])
        tk_ivt = vop(lambda e: e.reciprocal(out=binv[:], in_=brm[:]), waits=[tk_pmt])
        tk_ivt2 = vop(lambda e: e.tensor_scalar(out=binv[:], in0=binv[:], scalar1=448.0, scalar2=None, op0=ALU.mult), waits=[tk_ivt])
        tk_ta = vop(lambda e: e.tensor_scalar(out=bscg[:], in0=brm[:], scalar1=float(np.float32(C448_HI) * np.float32(0.25)), scalar2=None, op0=ALU.mult), waits=[tk_pmt])
        tk_tb = vop(lambda e: e.tensor_scalar(out=btmp[:], in0=brm[:], scalar1=float(np.float32(C448_LO) * np.float32(0.25)), scalar2=None, op0=ALU.mult), waits=[tk_pmt])
        tk_tsc = vop(lambda e: e.tensor_tensor(out=bscg[:], in0=bscg[:], in1=btmp[:], op=ALU.add), waits=[tk_ta, tk_tb])

        tst_fin = [tstore[-2], tstore[-1]]
        qttok = []
        for j in range(J):
            tc = tch[j % 2]
            w = tst_fin + [tk_ivt2] + ([qttok[j - 2][1]] if j >= 2 else [])
            tk_lt = load(tc[:], tbuf[j], f"tchld{j % 2}", waits=w)
            tku = vop(lambda e, tcc=tc: e.tensor_tensor(out=uch[:], in0=tcc[:], in1=binv[:], op=ALU.mult), waits=[tk_lt])
            tkp = vop(lambda e: e._custom_dve(QP_ACT, out=pch[:], in0=uch[:], s0=CDN, s1=CLAMP), waits=[tku])
            tkq = vop(lambda e, jj=j: e._custom_dve(QR_ACT, out=qt[jj][:], in0=uch[:], in1=pch[:], s0=MAGIC, s1=CDN), waits=[tkp])
            qttok.append((qt[j], tkq))

        for o in range(OD):
            bank = py[o % 2]
            wts = [py_free[o % 2]] if py_free[o % 2] else []
            wld0 = [dn_half_tok[0]] if dn_half_tok[0] else []
            wld1 = [dn_half_tok[1]] if dn_half_tok[1] else []
            tk_ld0 = load(slabd[0][:].rearrange("p (c e) -> p c e", e=128), cd[o, :, 0:22, :], "slabd0", waits=wld0)
            tk_ld1 = load(slabd[1][:].rearrange("p (c e) -> p c e", e=128), cd[o, :, 22:44, :], "slabd1", waits=wld1)
            tk_m = None
            for j in range(J):
                w_mm = []
                if j == 0:
                    w_mm = [tk_ld0, qttok[-1][1]] + wts
                elif j == 22:
                    w_mm = [tk_ld1]
                tk_m = top(lambda e, jj=j, bk=bank: e.matmul(bk[:], slabd[jj // 22][:, (jj % 22) * 128:((jj % 22) + 1) * 128], qt[jj][:], start=(jj == 0), stop=(jj == J - 1)),
                           waits=w_mm)
                if j == 21:
                    dn_half_tok[0] = tk_m
            dn_half_tok[1] = tk_m
            tk_ysc = vop(lambda e, oo=o, bk=bank: e.scalar_tensor_tensor(out=btmp[:], in0=bk[:], scalar=wsd[:, oo:oo + 1], in1=bscg[:], op0=ALU.mult, op1=ALU.mult), waits=[tk_m, tk_tsc])
            py_free[o % 2] = tk_ysc
            tk_x2 = vop(lambda e, oo=o: e.tensor_tensor(out=xn2[oo][:], in0=btmp[:], in1=x1[oo][:], op=ALU.add), waits=[tk_ysc])
            carry_out_st[o] = store(outT[o * 128:(o + 1) * 128, t0:t0 + T], xn2[o][:], "out", waits=[tk_x2])
        carry_out_st = [carry_out_st[OD - 1]] * OD
        carry_T_last = S.now(tt)
        carry_T_last = (tt, carry_T_last[1])

    final_out = S.now(dma_sem("out"))
    S.add("gpsimd", lambda e: e.nop(), waits=[final_out])

    with nc.Block() as block:
        @block.sync
        def _(eng):
            S.replay("sync", eng)

        @block.vector
        def _(eng):
            S.replay("vector", eng)

        @block.scalar
        def _(eng):
            S.replay("scalar", eng)

        @block.tensor
        def _(eng):
            S.replay("tensor", eng)

        @block.gpsimd
        def _(eng):
            eng.add_instruction(bass_isa.InstPseudoReloadLibraryIndex(
                lib_index=4, name=nc.get_next_instruction_name()))
            S.replay("gpsimd", eng)

    for h in reversed(ctx):
        h.__exit__(None, None, None)

    mybir.codegen_inst_isa_subclasses(nc)
    return nc


_NC_CACHE = None


def kernel(L0_attn_x, L0_attn_gamma, L0_attn_beta, L0_attn_W_in, L0_attn_W_out,
           L0_attn_dt, L0_ffn_w_gate, L0_ffn_w_up, L0_ffn_w_down,
           L0_ffn_up_reorder_index, L0_ffn_down_reorder_index):
    global _NC_CACHE
    import ml_dtypes
    x = np.asarray(L0_attn_x)
    dni = np.asarray(L0_ffn_down_reorder_index).astype(np.int64)
    WinT = np.ascontiguousarray(np.asarray(L0_attn_W_in).T)
    WoutT = np.ascontiguousarray(np.asarray(L0_attn_W_out).T)
    wg2 = np.ascontiguousarray(np.asarray(L0_ffn_w_gate)[dni, :])
    wu2 = np.ascontiguousarray(np.asarray(L0_ffn_w_up)[dni, :])
    wdn = np.ascontiguousarray(np.asarray(L0_ffn_w_down))
    gamT = np.ascontiguousarray(np.asarray(L0_attn_gamma).reshape(16, 128).T)
    betT = np.ascontiguousarray(np.asarray(L0_attn_beta).reshape(16, 128).T)
    dtT = np.ascontiguousarray(np.asarray(L0_attn_dt).reshape(32, 128).T)
    ident = np.eye(128, dtype=ml_dtypes.bfloat16)

    if _NC_CACHE is None:
        _NC_CACHE = build_nc()
    nc = _NC_CACHE

    in_maps = []
    for c in range(8):
        in_maps.append({
            "xT": np.ascontiguousarray(x[c].T),
            "WinT": WinT, "WoutT": WoutT,
            "wg2": wg2, "wu2": wu2, "wdn": wdn,
            "gamT": gamT, "betT": betT, "dtT": dtT,
            "identD": ident,
        })
    res = run_bass_kernel_spmd(nc, in_maps, core_ids=list(range(8)))
    out = np.empty((8, 2048, 2048), dtype=np.float32)
    for c in range(8):
        out[c] = np.asarray(res.results[c]["outT"]).T
    return out
